# version 01
# baseline (speedup 1.0000x reference)
import numpy as np
from contextlib import ExitStack
import concourse.bass as bass
import concourse.mybir as mybir
from concourse.bass_utils import run_bass_kernel_spmd

F32 = mybir.dt.float32
BF16 = mybir.dt.bfloat16
I32 = mybir.dt.int32
AF = mybir.ActivationFunctionType
ALU = mybir.AluOpType
AX = mybir.AxisListType

S = 2048
TQ = 1024
D = 4096
NH, NKV, HD = 32, 8, 128
DFF = 11008
DT = D // 128
FT = DFF // 128
NJ = 8
EJ = [2 * j + 2 for j in range(NJ)]
NTT = S // 128
NQT = TQ // 128
THETA = 500000.0
EPS = 1e-5
NEG = -1.0e30

ENGS = ("sync", "scalar", "vector", "tensor", "gpsimd")


class Plan:
    def __init__(self, sems):
        self.prog = {e: [] for e in ENGS}
        self.sems = sems
        self.cnt = {k: 0 for k in sems}
        self.enabled = True

    def em(self, eng, fn):
        if not self.enabled:
            return
        self.prog[eng].append(fn)

    def wait(self, eng, sem, val):
        if val <= 0 or not self.enabled:
            return
        s = self.sems[sem]
        self.em(eng, lambda e, s=s, v=val: e.wait_ge(s, v))

    def dma(self, eng, out, in_, sem, slow=False):
        if not self.enabled:
            return self.cnt[sem]
        self.cnt[sem] += 16
        v = self.cnt[sem]
        s = self.sems[sem]
        if slow:
            self.em(eng, lambda e, o=out, i=in_, s=s:
                    e.dma_start(out=o, in_=i, allow_slow_non_contiguous=True).then_inc(s, 16))
        else:
            self.em(eng, lambda e, o=out, i=in_, s=s: e.dma_start(out=o, in_=i).then_inc(s, 16))
        return v

    def op(self, eng, fn, sem=None):
        if not self.enabled:
            return self.cnt[sem] if sem else None
        if sem is None:
            self.em(eng, fn)
            return None
        self.cnt[sem] += 1
        v = self.cnt[sem]
        s = self.sems[sem]
        self.em(eng, lambda e, f=fn, s=s: f(e).then_inc(s, 1))
        return v

    def barrier(self, tag=""):
        if not self.enabled:
            return
        self.cnt["bar"] += len(ENGS)
        tgt = self.cnt["bar"]
        s = self.sems["bar"]
        for e in ENGS:
            self.em(e, lambda en, s=s: en.sem_inc(s, 1))
        for e in ENGS:
            self.em(e, lambda en, s=s, v=tgt: en.wait_ge(s, v))


def build(phases=("A", "B", "C"), debug=False):
    nc = bass.Bass("TRN2", target_bir_lowering=False)

    inp = {}
    def din(name, shape, dt):
        inp[name] = nc.dram_tensor(name, list(shape), dt, kind="ExternalInput").ap()
        return inp[name]

    xtm = din("xtm", (S, D), F32)
    xt = din("xt", (D, S), F32)
    xq = din("xq", (TQ, D), F32)
    xqt = din("xqt", (D, TQ), F32)
    wqh = din("wqh", (NH, 128, D), F32)
    wkh = din("wkh", (NKV, 128, D), F32)
    wvt = din("wvt", (D, NKV * HD), F32)
    woh = din("woh", (DT, 128, D), F32)
    gwh = din("gwh", (FT, 128, D), I32)
    uwh = din("uwh", (FT, 128, D), I32)
    dwh = din("dwh", (DT, 128, DFF), I32)
    gsh = din("gsh", (128, FT), F32)
    ush = din("ush", (128, FT), F32)
    dsh = din("dsh", (128, DT), F32)
    kcos = din("kcos", (128, S), F32)
    ksin = din("ksin", (128, S), F32)
    qcos = din("qcos", (128, TQ), F32)
    qsin = din("qsin", (128, TQ), F32)
    maskh = din("maskh", (NJ, 2, 128, 512), F32)
    onesr = din("onesr", (1, 128), F32)
    onesc = din("onesc", (128, 1), F32)

    out_fm = nc.dram_tensor("out_fm", [D, TQ], F32, kind="ExternalOutput").ap()

    skind = "ExternalOutput" if debug else "Internal"
    kfm_spill = nc.dram_tensor("kfm_spill", [NKV, 128, S], BF16, kind=skind).ap()
    vtm_spill = nc.dram_tensor("vtm_spill", [NKV, NTT, 128, 128], BF16, kind=skind).ap()
    q_spill = nc.dram_tensor("q_spill", [NKV, 128, 4 * TQ], BF16, kind=skind).ap()
    attn_spill = nc.dram_tensor("attn_spill", [NKV, NJ, 128, 512], BF16, kind=skind).ap()
    h_spill = nc.dram_tensor("h_spill", [FT, 128, TQ], BF16, kind=skind).ap()
    stats_hbm = nc.dram_tensor("stats_hbm", [4, S], F32, kind=skind).ap()
    dbg_xn = nc.dram_tensor("dbg_xn", [DT, 128, S], BF16, kind=skind).ap() if debug else None
    dbg_rb = nc.dram_tensor("dbg_rb", [2, 128, S], F32, kind=skind).ap() if debug else None
    dbg_rows = nc.dram_tensor("dbg_rows", [4, 6144], F32, kind=skind).ap() if debug else None
    dbg_xnq = nc.dram_tensor("dbg_xnq", [DT, 128, TQ], BF16, kind=skind).ap() if debug else None

    es = ExitStack()
    sems = {}
    for name in ("dw", "dx", "ds", "pe", "ac", "ve", "bar"):
        sems[name] = es.enter_context(nc.semaphore(name))
    P = Plan(sems)

    def SB(stack, n, shape, dt, side=None):
        return stack.enter_context(nc.sbuf_tensor(n, list(shape), dt, side=side))

    def PS(stack, n, shape, dt=F32):
        return stack.enter_context(nc.psum_tensor(n, list(shape), dt))

    ones_r = SB(es, "ones_r", (1, 128), F32, side="right")
    ones_c = SB(es, "ones_c", (128, 1), BF16, side="right")
    cst = SB(es, "cst", (128, 4), F32, side="right")
    P.op("vector", lambda e: e.memset(cst[:, 0:1], EPS))
    P.op("vector", lambda e: e.memset(cst[:, 1:2], -1.0))
    vcst = P.op("vector", lambda e: e.memset(cst[:, 2:3], 1.0 / D), sem="ve")
    es_rows = ExitStack()
    rows = SB(es_rows, "rows", (1, 6144), F32, side="right")
    R_RF, R_MF, R_RQ, R_MQ = 0, S, 2 * S, 2 * S + TQ
    v0 = P.dma("sync", ones_r[:], onesr, "dx")
    P.dma("gpsimd", ones_c[:], onesc, "dw")

    with ExitStack() as e0:
        sum_cols = SB(e0, "sum_cols", (128, 24), F32)
        sq_cols = SB(e0, "sq_cols", (128, 24), F32)
        r_cols = SB(e0, "r_cols", (128, 24), F32)
        mr_cols = SB(e0, "mr_cols", (128, 24), F32)
        sd_cols = SB(e0, "sd_cols", (128, 24), F32)
        xs_buf = [SB(e0, f"xs{i}", (128, D), F32) for i in range(2)]
        junk = SB(e0, "junk", (128, D), F32)
        slot_done_ac = [0, 0]
        for tt in range(NTT + NQT):
            sl = tt % 2
            src = xtm[tt * 128:(tt + 1) * 128, :] if tt < NTT else \
                xq[(tt - NTT) * 128:(tt - NTT + 1) * 128, :]
            P.wait("sync", "ac", slot_done_ac[sl])
            vd = P.dma("sync", xs_buf[sl][:], src, "dx")
            P.wait("scalar", "dx", vd)
            P.op("scalar", lambda e, b=xs_buf[sl], o=sum_cols[:, tt:tt + 1]:
                 e.activation(out=junk[:], in_=b[:], func=AF.Copy, accum_out=o))
            slot_done_ac[sl] = P.op(
                "scalar", lambda e, b=xs_buf[sl], o=sq_cols[:, tt:tt + 1]:
                e.activation(out=junk[:], in_=b[:], func=AF.Square, accum_out=o), sem="ac")
        P.wait("vector", "ac", slot_done_ac[(NTT + NQT - 1) % 2])
        P.wait("vector", "ve", vcst)
        P.op("vector", lambda e: e.tensor_scalar(sum_cols[:], sum_cols[:], cst[:, 2:3], None, ALU.mult))
        P.op("vector", lambda e: e.tensor_scalar(sq_cols[:], sq_cols[:], cst[:, 2:3], None, ALU.mult))
        P.op("vector", lambda e: e.tensor_mul(mr_cols[:], sum_cols[:], sum_cols[:]))
        vv = P.op("vector", lambda e: e.tensor_sub(sq_cols[:], sq_cols[:], mr_cols[:]), sem="ve")
        P.wait("scalar", "ve", vv)
        va2 = P.op("scalar", lambda e: e.activation(out=sd_cols[:], in_=sq_cols[:],
                                                    func=AF.Sqrt, bias=cst[:, 0:1]), sem="ac")
        P.wait("vector", "ac", va2)
        P.op("vector", lambda e: e.reciprocal(r_cols[:], sd_cols[:]))
        P.op("vector", lambda e: e.drain())
        vr = P.op("vector", lambda e: e.tensor_mul(mr_cols[:], sum_cols[:], r_cols[:]), sem="ve")
        P.wait("scalar", "ve", vr)
        for tt in range(NTT + NQT):
            u_r = 0 if tt < NTT else 2
            off = (tt if tt < NTT else tt - NTT) * 128
            P.dma("scalar", stats_hbm[u_r:u_r + 1, off:off + 128].rearrange("o t -> t o"),
                  r_cols[:, tt:tt + 1], "ds", slow=True)
            vlast = P.dma("scalar", stats_hbm[u_r + 1:u_r + 2, off:off + 128].rearrange("o t -> t o"),
                          mr_cols[:, tt:tt + 1], "ds", slow=True)
        P.wait("sync", "ds", vlast)
        P.dma("sync", rows[:, R_RF:R_RF + S], stats_hbm[0:1, :], "dx")
        P.dma("sync", rows[:, R_MF:R_MF + S], stats_hbm[1:2, :], "dx")
        P.dma("sync", rows[:, R_RQ:R_RQ + TQ], stats_hbm[2:3, 0:TQ], "dx")
        vrows = P.dma("sync", rows[:, R_MQ:R_MQ + TQ], stats_hbm[3:4, 0:TQ], "dx")
        P.wait("tensor", "dx", vrows)
        P.barrier("A0pre")
        if debug:
            P.dma("scalar", dbg_rows[3:4, :], rows[:, 0:6144], "ds")
            P.barrier("A0dbg")
        P.barrier("A0")

    with ExitStack() as e_xnf:
        xnf = SB(e_xnf, "xnf", (128, DT * S), BF16)

        P.enabled = ("A" in phases or "A1" in phases)
        with ExitStack() as e1:
            r_b = SB(e1, "r_b", (128, S), F32)
            mr_b = SB(e1, "mr_b", (128, S), F32)
            ps_bc = [PS(e1, f"ps_bc{i}", (128, 512)) for i in range(2)]
            P.wait("tensor", "dx", v0)
            evac_done = {}
            i = 0
            for dst, off, n in [(r_b, R_RF, S), (mr_b, R_MF, S)]:
                for ch in range(n // 512):
                    pb = ps_bc[i % 2]
                    if i >= 2:
                        P.wait("tensor", "ac", evac_done[i - 2])
                    pv = P.op("tensor", lambda e, pb=pb, o=off + ch * 512:
                              e.matmul(pb[:], ones_r[:], rows[:, o:o + 512], start=True, stop=True),
                              sem="pe")
                    P.wait("scalar", "pe", pv)
                    evac_done[i] = P.op("scalar", lambda e, pb=pb, d=dst, c=ch:
                                        e.copy(out=d[:, c * 512:(c + 1) * 512], in_=pb[:]), sem="ac")
                    i += 1
            P.wait("vector", "ac", evac_done[i - 1])
            if debug:
                P.wait("scalar", "ac", evac_done[i - 1])
                P.dma("scalar", dbg_rb[0], r_b[:], "ds")
                P.dma("scalar", dbg_rb[1], mr_b[:], "ds")

            xb = [SB(e1, f"xb{i}", (128, S), F32) for i in range(2)]
            tmul = SB(e1, "tmul", (128, S), F32)
            slot_ve = [0, 0]
            for dt in range(DT):
                sl = dt % 2
                P.wait("sync", "ve", slot_ve[sl])
                vx = P.dma("sync", xb[sl][:], xt[dt * 128:(dt + 1) * 128, :], "dx")
                P.wait("vector", "dx", vx)
                P.op("vector", lambda e, b=xb[sl]: e.tensor_mul(tmul[:], b[:], r_b[:]))
                slot_ve[sl] = P.op("vector", lambda e, d=dt:
                                   e.tensor_sub(xnf[:, d * S:(d + 1) * S], tmul[:], mr_b[:]), sem="ve")
                if debug:
                    P.wait("scalar", "ve", slot_ve[sl])
                    P.dma("scalar", dbg_xn[dt], xnf[:, dt * S:(dt + 1) * S], "ds")
            P.barrier("A1")
        P.enabled = True

        if "A" in phases:
            with ExitStack() as e2:
                wk_sb = [SB(e2, f"wk{i}", (128, D), BF16) for i in range(2)]
                kc_sb = SB(e2, "kc_sb", (128, S), F32)
                ks_sb = SB(e2, "ks_sb", (128, S), F32)
                rot = SB(e2, "rot", (128, S), F32)
                k1 = SB(e2, "k1", (128, S), F32)
                k_bf = SB(e2, "k_bf", (128, S), BF16)
                ps_k = PS(e2, "ps_k", (128, S))
                P.dma("sync", kc_sb[:], kcos, "dx")
                vr2 = P.dma("sync", ks_sb[:], ksin, "dx")
                wk_done = [0, 0]
                rope_done = 0
                kst_done = 0
                for kvt in range(NKV):
                    sl = kvt % 2
                    P.wait("gpsimd", "pe", wk_done[sl])
                    vw = P.dma("gpsimd", wk_sb[sl][:], wkh[kvt], "dw")
                    P.wait("tensor", "dw", vw)
                    P.wait("tensor", "ve", rope_done)
                    for dt in range(DT):
                        for ch in range(4):
                            pv = P.op("tensor", lambda e, sl=sl, d=dt, c=ch:
                                      e.matmul(ps_k[:, c * 512:(c + 1) * 512],
                                               wk_sb[sl][:, d * 128:(d + 1) * 128],
                                               xnf[:, d * S + c * 512:d * S + (c + 1) * 512],
                                               start=(d == 0), stop=(d == DT - 1)),
                                      sem="pe" if (dt == DT - 1 and ch == 3) else None)
                    wk_done[sl] = pv
                    P.wait("vector", "pe", pv)
                    if kvt == 0:
                        P.wait("vector", "dx", vr2)
                    P.wait("vector", "ds", kst_done)
                    P.op("vector", lambda e: e.tensor_copy(rot[0:64, :], ps_k[64:128, :]))
                    P.op("vector", lambda e: e.tensor_copy(rot[64:128, :], ps_k[0:64, :]))
                    P.op("vector", lambda e: e.tensor_mul(k1[:], ps_k[:], kc_sb[:]))
                    P.op("vector", lambda e: e.tensor_mul(rot[:], rot[:], ks_sb[:]))
                    rope_done = P.op("vector", lambda e: e.tensor_add(k_bf[:], k1[:], rot[:]), sem="ve")
                    P.wait("scalar", "ve", rope_done)
                    kst_done = P.dma("scalar", kfm_spill[kvt], k_bf[:], "ds")
                P.barrier("A2K")

            with ExitStack() as e3:
                wv_sb = SB(e3, "wv_sb", (128, DT * 512), BF16)
                v_bf = [SB(e3, f"v_bf{i}", (128, 512), BF16) for i in range(2)]
                ps_v = [PS(e3, f"ps_v{i}", (128, 512)) for i in range(2)]
                for vh in range(2):
                    P.barrier(f"A2V{vh}")
                    for dt in range(DT):
                        vw = P.dma("gpsimd", wv_sb[:, dt * 512:(dt + 1) * 512],
                                   wvt[dt * 128:(dt + 1) * 128, vh * 512:(vh + 1) * 512], "dw")
                    P.wait("tensor", "dw", vw)
                    ev_done = [0, 0]
                    st_done = [0, 0]
                    for tt in range(NTT):
                        sl = tt % 2
                        P.wait("tensor", "ac", ev_done[sl])
                        for dt in range(DT):
                            pv = P.op("tensor", lambda e, sl=sl, d=dt, t=tt:
                                      e.matmul(ps_v[sl][:],
                                               xnf[:, d * S + t * 128:d * S + (t + 1) * 128],
                                               wv_sb[:, d * 512:(d + 1) * 512],
                                               start=(d == 0), stop=(d == DT - 1)),
                                      sem="pe" if dt == DT - 1 else None)
                        P.wait("scalar", "pe", pv)
                        P.wait("scalar", "ds", st_done[sl])
                        ev_done[sl] = P.op("scalar", lambda e, sl=sl:
                                           e.copy(out=v_bf[sl][:], in_=ps_v[sl][:]), sem="ac")
                        st_done[sl] = P.dma("scalar",
                                            vtm_spill[vh * 4:(vh + 1) * 4, tt].rearrange("g p c -> p g c"),
                                            v_bf[sl][:].rearrange("p (g c) -> p g c", g=4), "ds")
                P.barrier("A2V")

    with ExitStack() as e_xnq:
        xnq = SB(e_xnq, "xnq", (128, DT * TQ), BF16)
        if "A" in phases:
            with ExitStack() as e4:
                rqb = SB(e4, "rqb", (128, TQ), F32)
                mrqb = SB(e4, "mrqb", (128, TQ), F32)
                ps_bc2 = PS(e4, "ps_bc2", (128, 512))
                ev = 0
                for dst, off in [(rqb, R_RQ), (mrqb, R_MQ)]:
                    for ch in range(TQ // 512):
                        P.wait("tensor", "ac", ev)
                        pv = P.op("tensor", lambda e, o=off + ch * 512:
                                  e.matmul(ps_bc2[:], ones_r[:], rows[:, o:o + 512],
                                           start=True, stop=True), sem="pe")
                        P.wait("scalar", "pe", pv)
                        ev = P.op("scalar", lambda e, d=dst, c=ch:
                                  e.copy(out=d[:, c * 512:(c + 1) * 512], in_=ps_bc2[:]), sem="ac")
                P.wait("vector", "ac", ev)
                xb2 = [SB(e4, f"xb2{i}", (128, TQ), F32) for i in range(2)]
                tmul2 = SB(e4, "tmul2", (128, TQ), F32)
                slot_ve = [0, 0]
                for dt in range(DT):
                    sl = dt % 2
                    P.wait("sync", "ve", slot_ve[sl])
                    vx = P.dma("sync", xb2[sl][:], xqt[dt * 128:(dt + 1) * 128, :], "dx")
                    P.wait("vector", "dx", vx)
                    P.op("vector", lambda e, b=xb2[sl]: e.tensor_mul(tmul2[:], b[:], rqb[:]))
                    slot_ve[sl] = P.op("vector", lambda e, d=dt:
                                       e.tensor_sub(xnq[:, d * TQ:(d + 1) * TQ], tmul2[:], mrqb[:]),
                                       sem="ve")
                    if debug:
                        P.wait("scalar", "ve", slot_ve[sl])
                        P.dma("scalar", dbg_xnq[dt], xnq[:, dt * TQ:(dt + 1) * TQ], "ds")
                P.barrier("A3x")

                wq_sb = [SB(e4, f"wq{i}", (128, D), BF16) for i in range(2)]
                qc_sb = SB(e4, "qc_sb", (128, TQ), F32)
                qs_sb = SB(e4, "qs_sb", (128, TQ), F32)
                rotq = SB(e4, "rotq", (128, TQ), F32)
                q1 = SB(e4, "q1", (128, TQ), F32)
                q_bf = SB(e4, "q_bf", (128, TQ), BF16)
                ps_q = PS(e4, "ps_q", (128, TQ))
                P.dma("sync", qc_sb[:], qcos, "dx")
                vr2 = P.dma("sync", qs_sb[:], qsin, "dx")
                wq_done = [0, 0]
                rope_done = 0
                qst_done = 0
                for qdt in range(NH):
                    sl = qdt % 2
                    P.wait("gpsimd", "pe", wq_done[sl])
                    vw = P.dma("gpsimd", wq_sb[sl][:], wqh[qdt], "dw")
                    P.wait("tensor", "dw", vw)
                    P.wait("tensor", "ve", rope_done)
                    for dt in range(DT):
                        for ch in range(2):
                            pv = P.op("tensor", lambda e, sl=sl, d=dt, c=ch:
                                      e.matmul(ps_q[:, c * 512:(c + 1) * 512],
                                               wq_sb[sl][:, d * 128:(d + 1) * 128],
                                               xnq[:, d * TQ + c * 512:d * TQ + (c + 1) * 512],
                                               start=(d == 0), stop=(d == DT - 1)),
                                      sem="pe" if (dt == DT - 1 and ch == 1) else None)
                    wq_done[sl] = pv
                    P.wait("vector", "pe", pv)
                    if qdt == 0:
                        P.wait("vector", "dx", vr2)
                    P.wait("vector", "ds", qst_done)
                    P.op("vector", lambda e: e.tensor_copy(rotq[0:64, :], ps_q[64:128, :]))
                    P.op("vector", lambda e: e.tensor_copy(rotq[64:128, :], ps_q[0:64, :]))
                    P.op("vector", lambda e: e.tensor_mul(q1[:], ps_q[:], qc_sb[:]))
                    P.op("vector", lambda e: e.tensor_mul(rotq[:], rotq[:], qs_sb[:]))
                    rope_done = P.op("vector", lambda e: e.tensor_add(q_bf[:], q1[:], rotq[:]), sem="ve")
                    P.wait("scalar", "ve", rope_done)
                    qst_done = P.dma("scalar", q_spill[qdt // 4][:, (qdt % 4) * TQ:(qdt % 4 + 1) * TQ],
                                     q_bf[:], "ds")
                P.barrier("A3Q")

        if "B" in phases:
            with ExitStack() as eB:
                mask_sb = SB(eB, "mask_sb", (128, NJ * 2 * 512), F32)
                k_sb = [SB(eB, f"k_sb{i}", (128, S), BF16) for i in range(2)]
                vt_sb = [SB(eB, f"vt_sb{i}", (128, S), BF16) for i in range(2)]
                q4_sb = [SB(eB, f"q4_sb{i}", (128, 4 * TQ), BF16) for i in range(2)]
                p_sb = SB(eB, "p_sb", (128, 16 * 512), BF16)
                t_ln = SB(eB, "t_ln", (1, 512), F32)
                invd = SB(eB, "invd", (1, 512), F32)
                invd_b = SB(eB, "invd_b", (128, 512), F32)
                attn_ev = [SB(eB, f"attn_ev{i}", (128, 512), BF16) for i in range(2)]
                ps_s = [PS(eB, f"ps_s{i}", (128, 1024)) for i in range(2)]
                ps_av = PS(eB, "ps_av", (128, 512))
                ps_den = PS(eB, "ps_den", (1, 512))
                ps_bc3 = PS(eB, "ps_bc3", (128, 512))
                vm = P.dma("sync", mask_sb[:].rearrange("p (j u c) -> p j u c", j=NJ, u=2),
                           maskh.rearrange("j u p c -> p j u c"), "dx")
                P.wait("vector", "dx", vm)
                kv_done = [0, 0]
                ev_mul_done = 0
                st_attn = [0, 0]
                strip_ac = [0, 0]
                for kvh in range(NKV):
                    sl = kvh % 2
                    P.wait("sync", "pe", kv_done[sl])
                    vk = P.dma("sync", k_sb[sl][:], kfm_spill[kvh], "dx")
                    vv2 = P.dma("sync", vt_sb[sl][:].rearrange("p (k c) -> p k c", k=NTT),
                                vtm_spill[kvh].rearrange("k p c -> p k c"), "dx")
                    vq = P.dma("sync", q4_sb[sl][:], q_spill[kvh], "dx")
                    P.wait("tensor", "dx", vq)
                    for j in range(NJ):
                        e_j = EJ[j]
                        exp_done = {}
                        for g in range((e_j + 1) // 2):
                            kts = [2 * g + u for u in range(2) if 2 * g + u < e_j]
                            strip = ps_s[g % 2]
                            P.wait("tensor", "ac", strip_ac[g % 2])
                            for u, kt in enumerate(kts):
                                mv = q4_sb[sl][:].rearrange("p (i t) -> p i t", i=4)[:, :, j * 128:(j + 1) * 128]
                                pv = P.op("tensor", lambda e, sl=sl, kt=kt, u=u, strip=strip, mv=mv:
                                          e.matmul(strip[:, u * 512:(u + 1) * 512],
                                                   k_sb[sl][:, kt * 128:(kt + 1) * 128], mv,
                                                   start=True, stop=True),
                                          sem="pe" if u == len(kts) - 1 else None)
                            masked = [u for u, kt in enumerate(kts) if kt >= 2 * j]
                            vmask = None
                            if masked:
                                P.wait("vector", "pe", pv)
                                for u in masked:
                                    kt = 2 * g + u
                                    moff = (j * 2 + (kt - 2 * j)) * 512
                                    vmask = P.op("vector", lambda e, strip=strip, u=u, mo=moff:
                                                 e.tensor_add(strip[:, u * 512:(u + 1) * 512],
                                                              strip[:, u * 512:(u + 1) * 512],
                                                              mask_sb[:, mo:mo + 512]), sem="ve")
                            P.wait("scalar", "pe", pv)
                            if vmask is not None:
                                P.wait("scalar", "ve", vmask)
                            span = len(kts) * 512
                            exp_done[g] = P.op("scalar", lambda e, strip=strip, g=g, span=span:
                                               e.activation(out=p_sb[:, g * 1024:g * 1024 + span],
                                                            in_=strip[:, 0:span], func=AF.Exp),
                                               sem="ac")
                            strip_ac[g % 2] = exp_done[g]
                        P.wait("tensor", "ac", exp_done[(e_j - 1) // 2])
                        P.wait("tensor", "ve", ev_mul_done)
                        for kt in range(e_j):
                            P.op("tensor", lambda e, kt=kt, e_j=e_j:
                                 e.matmul(ps_den[:], ones_c[:], p_sb[:, kt * 512:(kt + 1) * 512],
                                          start=(kt == 0), stop=(kt == e_j - 1)))
                            pv2 = P.op("tensor", lambda e, sl=sl, kt=kt, e_j=e_j:
                                       e.matmul(ps_av[:], vt_sb[sl][:, kt * 128:(kt + 1) * 128],
                                                p_sb[:, kt * 512:(kt + 1) * 512],
                                                start=(kt == 0), stop=(kt == e_j - 1)),
                                       sem="pe" if kt == e_j - 1 else None)
                        kv_done[sl] = pv2
                        P.wait("scalar", "pe", pv2)
                        P.op("scalar", lambda e: e.activation(out=t_ln[:], in_=ps_den[:], func=AF.Ln))
                        vinv = P.op("scalar", lambda e: e.activation(out=invd[:], in_=t_ln[:],
                                                                     func=AF.Exp, scale=cst[0:1, 1:2]), sem="ac")
                        P.wait("tensor", "ac", vinv)
                        pv3 = P.op("tensor", lambda e: e.matmul(ps_bc3[:], ones_r[:], invd[:],
                                                                start=True, stop=True), sem="pe")
                        P.wait("scalar", "pe", pv3)
                        vib = P.op("scalar", lambda e: e.copy(out=invd_b[:], in_=ps_bc3[:]), sem="ac")
                        asl = (kvh * NJ + j) % 2
                        P.wait("vector", "ac", vib)
                        P.wait("vector", "ds", st_attn[asl])
                        ev_mul_done = P.op("vector", lambda e, asl=asl:
                                           e.tensor_mul(attn_ev[asl][:], ps_av[:], invd_b[:]), sem="ve")
                        P.wait("scalar", "ve", ev_mul_done)
                        st_attn[asl] = P.dma("scalar", attn_spill[kvh, j], attn_ev[asl][:], "ds")
                P.barrier("B")

        if "C" in phases:
            with ExitStack() as eC:
                g_sb = [SB(eC, f"g_sb{i}", (128, D), BF16) for i in range(2)]
                u_sb = [SB(eC, f"u_sb{i}", (128, D), BF16) for i in range(2)]
                gs_sb = SB(eC, "gs_sb", (128, FT), F32)
                us_sb = SB(eC, "us_sb", (128, FT), F32)
                t1 = [SB(eC, f"t1_{i}", (128, TQ), F32) for i in range(2)]
                t2 = [SB(eC, f"t2_{i}", (128, TQ), F32) for i in range(2)]
                h_bf = [SB(eC, f"h_bf{i}", (128, TQ), BF16) for i in range(2)]
                ps_g = [PS(eC, f"ps_g{i}", (128, TQ)) for i in range(2)]
                ps_u = [PS(eC, f"ps_u{i}", (128, TQ)) for i in range(2)]
                P.dma("sync", gs_sb[:], gsh, "dx")
                vs2 = P.dma("sync", us_sb[:], ush, "dx")
                P.wait("scalar", "dx", vs2)
                w_done = [0, 0]
                hmul_done = [0, 0]
                hst_done = [0, 0]
                ep_done = [0, 0]
                for ft in range(FT):
                    sl = ft % 2
                    P.wait("gpsimd", "pe", w_done[sl])
                    P.dma("gpsimd", g_sb[sl][:], gwh[ft], "dw")
                    vw = P.dma("gpsimd", u_sb[sl][:], uwh[ft], "dw")
                    P.wait("tensor", "dw", vw)
                    P.wait("tensor", "ve", hmul_done[sl])
                    P.wait("tensor", "ac", ep_done[sl])
                    for dt in range(DT):
                        for ch in range(2):
                            P.op("tensor", lambda e, sl=sl, d=dt, c=ch:
                                 e.matmul(ps_g[sl][:, c * 512:(c + 1) * 512],
                                          g_sb[sl][:, d * 128:(d + 1) * 128],
                                          xnq[:, d * TQ + c * 512:d * TQ + (c + 1) * 512],
                                          start=(d == 0), stop=(d == DT - 1)))
                        for ch in range(2):
                            pv = P.op("tensor", lambda e, sl=sl, d=dt, c=ch:
                                      e.matmul(ps_u[sl][:, c * 512:(c + 1) * 512],
                                               u_sb[sl][:, d * 128:(d + 1) * 128],
                                               xnq[:, d * TQ + c * 512:d * TQ + (c + 1) * 512],
                                               start=(d == 0), stop=(d == DT - 1)),
                                      sem="pe" if (dt == DT - 1 and ch == 1) else None)
                    w_done[sl] = pv
                    P.wait("scalar", "pe", pv)
                    P.wait("scalar", "ve", hmul_done[sl])
                    P.op("scalar", lambda e, sl=sl, f=ft:
                         e.activation(out=t1[sl][:], in_=ps_g[sl][:], func=AF.Silu, scale=gs_sb[:, f:f + 1]))
                    ep_done[sl] = P.op("scalar", lambda e, sl=sl, f=ft:
                                       e.activation(out=t2[sl][:], in_=ps_u[sl][:], func=AF.Copy,
                                                    scale=us_sb[:, f:f + 1]), sem="ac")
                    P.wait("vector", "ac", ep_done[sl])
                    P.wait("vector", "ds", hst_done[sl])
                    hmul_done[sl] = P.op("vector", lambda e, sl=sl:
                                         e.tensor_mul(h_bf[sl][:], t1[sl][:], t2[sl][:]), sem="ve")
                    P.wait("scalar", "ve", hmul_done[sl])
                    hst_done[sl] = P.dma("scalar", h_spill[ft], h_bf[sl][:], "ds")
                P.barrier("C1")

    if "C" in phases:
        es_rows.close()
        with ExitStack() as eD:
            h_half = SB(eD, "h_half", (128, FT * 512), BF16)
            attn_half = SB(eD, "attn_half", (128, DT * 512), BF16)
            d_sb = [SB(eD, f"d_sb{i}", (128, DFF), BF16) for i in range(2)]
            wo_sb = [SB(eD, f"wo_sb{i}", (128, D), BF16) for i in range(2)]
            ds_sb = SB(eD, "ds_sb", (128, DT), F32)
            xq_t = [SB(eD, f"xq_t{i}", (128, 512), F32) for i in range(2)]
            t3 = [SB(eD, f"t3_{i}", (128, 512), F32) for i in range(2)]
            t4 = [SB(eD, f"t4_{i}", (128, 512), F32) for i in range(2)]
            out_sb = [SB(eD, f"out_sb{i}", (128, 512), F32) for i in range(2)]
            ps_d = [PS(eD, f"ps_d{i}", (128, 512)) for i in range(2)]
            ps_o = [PS(eD, f"ps_o{i}", (128, 512)) for i in range(2)]
            vds = P.dma("sync", ds_sb[:], dsh, "dx")
            P.wait("scalar", "dx", vds)
            for hf in range(2):
                P.barrier(f"C2{hf}")
                for ftt in range(FT):
                    P.dma("sync", h_half[:, ftt * 512:(ftt + 1) * 512],
                          h_spill[ftt][:, hf * 512:(hf + 1) * 512], "dx")
                for kvh in range(NKV):
                    for jj in range(4):
                        j = 4 * hf + jj
                        vh2 = P.dma("sync",
                                    attn_half[:].rearrange("p (c q) -> p c q", c=DT)
                                    [:, 4 * kvh:4 * kvh + 4, jj * 128:(jj + 1) * 128],
                                    attn_spill[kvh, j].rearrange("p (i q) -> p i q", i=4), "dx")
                w_done = [0, 0]
                ev_done = [0, 0]
                st_done = [0, 0]
                for mt in range(DT):
                    sl = mt % 2
                    P.wait("gpsimd", "pe", w_done[sl])
                    P.dma("gpsimd", d_sb[sl][:], dwh[mt], "dw")
                    vw = P.dma("gpsimd", wo_sb[sl][:], woh[mt], "dw")
                    P.wait("sync", "ve", ev_done[sl])
                    vx = P.dma("sync", xq_t[sl][:], xqt[mt * 128:(mt + 1) * 128, hf * 512:(hf + 1) * 512],
                               "dx")
                    P.wait("tensor", "dw", vw)
                    if mt == 0:
                        P.wait("tensor", "dx", vh2)
                    P.wait("tensor", "ve", ev_done[sl])
                    for ftt in range(FT):
                        P.op("tensor", lambda e, sl=sl, f=ftt:
                             e.matmul(ps_d[sl][:], d_sb[sl][:, f * 128:(f + 1) * 128],
                                      h_half[:, f * 512:(f + 1) * 512],
                                      start=(f == 0), stop=(f == FT - 1)))
                    for ct in range(DT):
                        pv = P.op("tensor", lambda e, sl=sl, c=ct:
                                  e.matmul(ps_o[sl][:], wo_sb[sl][:, c * 128:(c + 1) * 128],
                                           attn_half[:, c * 512:(c + 1) * 512],
                                           start=(c == 0), stop=(c == DT - 1)),
                                  sem="pe" if ct == DT - 1 else None)
                    w_done[sl] = pv
                    P.wait("scalar", "pe", pv)
                    P.wait("scalar", "ve", ev_done[sl])
                    va3 = P.op("scalar", lambda e, sl=sl, m=mt:
                               e.activation(out=t3[sl][:], in_=ps_d[sl][:], func=AF.Copy,
                                            scale=ds_sb[:, m:m + 1]), sem="ac")
                    P.wait("vector", "ac", va3)
                    P.wait("vector", "dx", vx)
                    P.wait("vector", "ds", st_done[sl])
                    P.op("vector", lambda e, sl=sl: e.tensor_add(t4[sl][:], ps_o[sl][:], t3[sl][:]))
                    ev_done[sl] = P.op("vector", lambda e, sl=sl:
                                       e.tensor_add(out_sb[sl][:], t4[sl][:], xq_t[sl][:]), sem="ve")
                    P.wait("scalar", "ve", ev_done[sl])
                    st_done[sl] = P.dma("scalar",
                                        out_fm[mt * 128:(mt + 1) * 128, hf * 512:(hf + 1) * 512],
                                        out_sb[sl][:], "ds")
            P.barrier("C2")

    P.wait("sync", "ds", P.cnt["ds"])
    P.wait("sync", "dw", P.cnt["dw"])
    P.wait("sync", "dx", P.cnt["dx"])

    es_rows.close()
    block = es.enter_context(nc.Block())
    for eng in ENGS:
        fns = P.prog[eng]
        getattr(block, eng)(lambda e, fns=fns: [f(e) for f in fns])
    es.close()
    return nc


def host_tiles(WT, Kt, Mt):
    K, M = WT.shape
    assert K == Kt * 128 and M == Mt * 128
    return np.ascontiguousarray(WT.reshape(Kt, 128, Mt, 128).transpose(2, 1, 0, 3).reshape(Mt, 128, Kt * 128))


def rope_tables():
    inv = 1.0 / THETA ** (np.arange(0, HD, 2, dtype=np.float64) / HD)
    t = np.arange(S, dtype=np.float64)
    fr = np.outer(t, inv)
    emb = np.concatenate([fr, fr], axis=-1)
    return np.cos(emb).astype(np.float32), np.sin(emb).astype(np.float32)


def qrows_for(hpar):
    return np.concatenate([np.arange((2 * j + hpar) * 128, (2 * j + hpar + 1) * 128) for j in range(NJ)])


def make_masks(hpar):
    m = np.zeros((NJ, 2, 128, 512), np.float32)
    for j in range(NJ):
        g = 2 * j + hpar
        qg = g * 128 + np.arange(128)
        for u in range(2):
            kt = 2 * j + u
            kg = kt * 128 + np.arange(128)
            blk = np.where(kg[:, None] <= qg[None, :], 0.0, NEG).astype(np.float32)
            m[j, u] = np.tile(blk, (1, 4))
    return m


def prep_inputs(full):
    x = np.asarray(full["L0_attn_x"], np.float32)
    wq = np.asarray(full["L0_attn_wq"], np.float32)
    wk = np.asarray(full["L0_attn_wk"], np.float32)
    wv = np.asarray(full["L0_attn_wv"], np.float32)
    wo = np.asarray(full["L0_attn_wo"], np.float32)
    gw = np.asarray(full["L0_ffn_gate_w"], np.int32)
    uw = np.asarray(full["L0_ffn_up_w"], np.int32)
    dw = np.asarray(full["L0_ffn_down_w"], np.int32)
    gs = np.asarray(full["L0_ffn_gate_s"], np.float32).reshape(-1)
    us = np.asarray(full["L0_ffn_up_s"], np.float32).reshape(-1)
    ds = np.asarray(full["L0_ffn_down_s"], np.float32).reshape(-1)

    wqh = host_tiles(np.ascontiguousarray(wq.T), DT, NH)
    wkh = host_tiles(np.ascontiguousarray(wk.T), DT, NKV)
    wvt = np.ascontiguousarray(wv.T)
    woh = host_tiles(np.ascontiguousarray(wo.T), DT, DT)
    gwh = host_tiles(np.ascontiguousarray(gw.T), DT, FT)
    uwh = host_tiles(np.ascontiguousarray(uw.T), DT, FT)
    dwh = host_tiles(np.ascontiguousarray(dw.T), FT, DT)
    gsh = np.ascontiguousarray(gs.reshape(FT, 128).T)
    ush = np.ascontiguousarray(us.reshape(FT, 128).T)
    dsh = np.ascontiguousarray(ds.reshape(DT, 128).T)
    cos, sin = rope_tables()
    kcosA = np.ascontiguousarray(cos.T)
    sinT = sin.T.copy()
    sinT[0:64, :] *= -1.0
    ksinA = np.ascontiguousarray(sinT)
    onesr = np.ones((1, 128), np.float32)
    onesc = np.ones((128, 1), np.float32)

    in_maps = []
    sc = 1.0 / np.sqrt(HD)
    for c in range(8):
        b, hpar = c // 2, c % 2
        qr = qrows_for(hpar)
        xb = np.ascontiguousarray(x[b])
        xqm = np.ascontiguousarray(xb[qr])
        im = dict(
            xtm=xb, xt=np.ascontiguousarray(xb.T),
            xq=xqm, xqt=np.ascontiguousarray(xqm.T),
            wqh=wqh, wkh=wkh, wvt=wvt, woh=woh,
            gwh=gwh, uwh=uwh, dwh=dwh, gsh=gsh, ush=ush, dsh=dsh,
            kcos=kcosA, ksin=ksinA,
            qcos=np.ascontiguousarray(kcosA[:, qr] * sc),
            qsin=np.ascontiguousarray(ksinA[:, qr] * sc),
            maskh=make_masks(hpar), onesr=onesr, onesc=onesc,
        )
        in_maps.append(im)
    return in_maps


def kernel(**inputs):
    in_maps = prep_inputs(inputs)
    nc = build()
    res = run_bass_kernel_spmd(nc, in_maps, core_ids=list(range(8)))
    x = np.asarray(inputs["L0_attn_x"], np.float32)
    out = np.empty_like(x)
    for c in range(8):
        b, hpar = c // 2, c % 2
        qr = qrows_for(hpar)
        out[b][qr] = np.asarray(res.results[c]["out_fm"]).T
    return out
